# version 17
# baseline (speedup 1.0000x reference)
import contextlib
import numpy as np

import concourse.bass as bass
import concourse.mybir as mybir
from concourse.bass_utils import run_bass_kernel_spmd

F32 = mybir.dt.float32
BF16 = mybir.dt.bfloat16
AF = mybir.ActivationFunctionType
ALU = mybir.AluOpType
AX = mybir.AxisListType.X

B, S, D, H = 2, 4096, 4096, 32
NDC = 32
G = 512
SCALE = float(128) ** -0.5
SLOPES = [2.0 ** (-0.25 * (i + 1)) for i in range(H)]
RANKS = (8, 16, 32, 64)
INV_D = 1.0 / D
EPS = 1e-5
PHASE = "full"
TAPS = False


class Prog:
    ENGS = ("t", "v", "s", "y", "g")
    SEMS = ("t", "v", "s", "y", "g", "gx0", "gx1", "w0", "w1")

    def __init__(self, nc):
        self.nc = nc
        self.q = {e: [] for e in self.ENGS}
        self.cnt = {e: 0 for e in self.SEMS}
        self.lastw = {}
        self.sems = {}

    def handle(self, e):
        nc = self.nc
        return {"t": nc.tensor, "v": nc.vector, "s": nc.scalar,
                "y": nc.sync, "g": nc.gpsimd}[e]

    def em(self, e, fn):
        self.q[e].append(("op", fn))

    def sig(self, e):
        self.cnt[e] += 1
        v = self.cnt[e]
        self.q[e].append(("sig", v))
        return (e, v)

    def dma(self, e, out, in_, stream=None):
        st = stream or e
        self.cnt[st] += 16
        v = self.cnt[st]
        self.q[e].append(("dma", out, in_, v, st))
        return (st, v)

    def wait(self, e, ev):
        if ev is None:
            return
        pe, v = ev
        if pe == e and pe in ("t", "v", "s"):
            return
        if self.lastw.get((e, pe), 0) >= v:
            return
        self.lastw[(e, pe)] = v
        self.q[e].append(("wait", pe, v))

    def emit_all(self):
        nc = self.nc
        with nc.Block() as blk:
            decs = {"t": blk.tensor, "v": blk.vector, "s": blk.scalar,
                    "y": blk.sync, "g": blk.gpsimd}
            for e in self.ENGS:
                def section(eng, e=e):
                    h = self.handle(e)
                    for item in self.q[e]:
                        kind = item[0]
                        if kind == "op":
                            item[1](h)
                        elif kind == "sig":
                            h.drain().then_inc(self.sems[e], 1)
                        elif kind == "dma":
                            h.dma_start(out=item[1], in_=item[2]).then_inc(
                                self.sems[item[4]], 16)
                        elif kind == "wait":
                            h.wait_ge(self.sems[item[1]], item[2])
                decs[e](section)


def build_program():
    nc = bass.Bass("TRN2", target_bir_lowering=False)
    dt = nc.dram_tensor

    xg_in = dt("xg", [2, 128, NDC * G], F32, kind="ExternalInput").ap()
    xkv_in = dt("xkv", [128, 8, NDC * G], F32, kind="ExternalInput").ap()
    wq_in = dt("wq", [32, 128, NDC * 128], F32, kind="ExternalInput").ap()
    wo_in = dt("wo", [32, 128, NDC * 128], F32, kind="ExternalInput").ap()
    wf_in = dt("wf", [32, 128, NDC * 128], F32, kind="ExternalInput").ap()
    wkv_in = dt("wkv", [128, NDC * 256], F32, kind="ExternalInput").ap()
    at_in = dt("at", [128, NDC * 64], F32, kind="ExternalInput").ap()
    bt_in = dt("bt", [64, 4096], F32, kind="ExternalInput").ap()
    gw_in = dt("gw", [128, NDC * 4], F32, kind="ExternalInput").ap()
    bias_in = dt("fb", [128, 32], F32, kind="ExternalInput").ap()
    gb_in = dt("gb", [128, 4], F32, kind="ExternalInput").ap()
    kpos_in = dt("kpos", [128, 4096], F32, kind="ExternalInput").ap()
    eye_in = dt("eye", [128, 128], F32, kind="ExternalInput").ap()
    dmask_in = dt("dmask", [128, 512], F32, kind="ExternalInput").ap()
    qpos_in = dt("qpos", [128, 8], F32, kind="ExternalInput").ap()
    yt_out = dt("yt", [2, 128, NDC * G], F32, kind="ExternalOutput").ap()
    if PHASE == "kv":
        kt_out = dt("kt_o", [128, 4096], F32, kind="ExternalOutput").ap()
        vn_out = dt("vn_o", [128, 4096], F32, kind="ExternalOutput").ap()
    if TAPS:
        qt_o = dt("qt_o", [128, NDC * G], F32, kind="ExternalOutput").ap()
        de_o = dt("de_o", [4, 128, 1], F32, kind="ExternalOutput").ap()
        dg_o = dt("dg_o", [128, 256], F32, kind="ExternalOutput").ap()
        p_o = dt("p_o", [128, 4096], F32, kind="ExternalOutput").ap()
        oh_o = dt("oh_o", [1, 2048], F32, kind="ExternalOutput").ap()
        u_o = dt("u_o", [64, 512], F32, kind="ExternalOutput").ap()
        ue_o = dt("ue_o", [64, 2048], F32, kind="ExternalOutput").ap()
        y_o = dt("y_o", [128, NDC * G], F32, kind="ExternalOutput").ap()
        x2_o = dt("x2_o", [128, NDC * G], F32, kind="ExternalOutput").ap()
        g_o = dt("g_o", [128, 8], F32, kind="ExternalOutput").ap()
        rc_o = dt("rc_o", [2, 128, 1], F32, kind="ExternalOutput").ap()
        oc_o = dt("oc_o", [4, 128, 1], F32, kind="ExternalOutput").ap()
        ot_o = dt("ot_o", [128, 16], F32, kind="ExternalOutput").ap()
        at_o = dt("at_o", [128, NDC * G], F32, kind="ExternalOutput").ap()
        xr_o = dt("xr_o", [128, NDC * G], F32, kind="ExternalOutput").ap()

    P = Prog(nc)

    with contextlib.ExitStack() as ctx:
        E = ctx.enter_context
        POOL = 94656
        pool_h = E(nc.sbuf_tensor("pool", [128, POOL], BF16))
        pool = pool_h[:, :]
        cur = [0]

        def ab(cols, align=64):
            a = (cur[0] + align - 1) // align * align
            cur[0] = a + cols
            assert cur[0] <= POOL, f"pool overflow {cur[0]}"
            return pool[:, a:a + cols]

        def af(cols_f32, align=64):
            return ab(2 * cols_f32, align).bitcast(F32)

        KT = ab(4096)
        Vn = ab(4096)
        KPOS = af(4096)
        dmask = ab(512)
        eyeb = ab(128)
        qpos = af(8)
        wbuf = ab(8192)
        XRs = ab(32768)
        XAs = ab(16384)
        QYs = ab(16384)
        atb = wbuf[:, 0:NDC * 64]
        gwf = af(NDC * 4)
        biast = af(32)
        gbb = af(4)
        ncsb = af(4)
        tmpc = ab(1024)
        tmp = tmpc.bitcast(F32)
        rbc = af(512)
        mrbc = af(512)
        ysq = tmpc[:, 0:512]

        XRf = XRs.bitcast(F32)
        s1v = [XRs[:, 0:8192].bitcast(F32), XRs[:, 8192:16384].bitcast(F32)]
        pv = [XRs[:, 16384:20480], XRs[:, 20480:24576]]
        pts = [XRs[:, 24576:26624], XRs[:, 26624:28672]]
        xbK = [XRs[:, 0:16384], XRs[:, 16384:32768]]
        vtmp = XAs[:, 0:512]

        btb = E(nc.sbuf_tensor("btb", [64, 4096], BF16))
        u_sb = E(nc.sbuf_tensor("u_sb", [64, 512], BF16))
        ue = E(nc.sbuf_tensor("ue", [64, 4 * 512], BF16))
        ohT = E(nc.sbuf_tensor("ohT", [1, 4 * 512], BF16))
        rows = E(nc.sbuf_tensor("rows", [1, 2 * 512], F32))
        csr = E(nc.sbuf_tensor("csr", [1, 4], F32))
        diag = E(nc.sbuf_tensor("diag", [128, 256], BF16))
        den = [E(nc.sbuf_tensor(f"den{i}", [128, 1], F32)) for i in range(2)]
        rden = [E(nc.sbuf_tensor(f"rden{i}", [128, 1], F32)) for i in range(2)]
        ebias = [E(nc.sbuf_tensor(f"ebias{i}", [128, 1], F32)) for i in range(2)]
        Gs = E(nc.sbuf_tensor("Gs", [128, 4], F32))
        Gf = E(nc.sbuf_tensor("Gf", [128, 4], F32))
        gmx = E(nc.sbuf_tensor("gmx", [128, 1], F32))
        oh = E(nc.sbuf_tensor("oh", [128, 4], F32))
        ohc4 = [E(nc.sbuf_tensor(f"ohc{i}", [128, 1], BF16)) for i in range(4)]
        r2a = E(nc.sbuf_tensor("r2a", [128, 1], F32))
        r2b = E(nc.sbuf_tensor("r2b", [128, 1], F32))
        onesb = E(nc.sbuf_tensor("onesb", [128, 1], BF16))
        ones128f = E(nc.sbuf_tensor("ones128f", [128, 1], F32))
        ones1 = E(nc.sbuf_tensor("ones1", [1, 128], F32))
        one11 = E(nc.sbuf_tensor("one11", [1, 1], F32))
        ones64b = E(nc.sbuf_tensor("ones64b", [1, 64], BF16))
        epsv = E(nc.sbuf_tensor("epsv", [128, 1], F32))
        zz = E(nc.sbuf_tensor("zz", [128, 1], F32))
        csneg = [E(nc.sbuf_tensor(f"csneg{i}", [128, 1], F32)) for i in range(2)]

        PS = [E(nc.psum_tensor(f"ps{i}", [128, 1024], F32)) for i in range(4)]

        for e in Prog.SEMS:
            P.sems[e] = E(nc.semaphore(f"sem_{e}"))

        m_row = rows[0:1, 0:512]
        var_row = rows[0:1, 512:1024]
        r_row = rows[0:1, 512:1024]
        mr_row = rows[0:1, 0:512]

        def vop(fn):
            P.em("v", fn)

        ld = []
        ld.append(P.dma("g", eyeb[:, :], eye_in))
        ld.append(P.dma("g", btb[:, :], bt_in))
        ld.append(P.dma("y", KPOS[:, :], kpos_in))
        ld.append(P.dma("g", dmask[:, :], dmask_in))
        ld.append(P.dma("y", qpos[:, :], qpos_in))
        ld.append(P.dma("y", gwf[:, :], gw_in))
        ld.append(P.dma("y", biast[:, :], bias_in))
        ld.append(P.dma("y", gbb[:, :], gb_in))
        ev_wkv = P.dma("g", wbuf[:, 0:8192], wkv_in)

        vop(lambda v: v.memset(onesb[:, :], 1.0))
        vop(lambda v: v.memset(ones128f[:, :], 1.0))
        vop(lambda v: v.memset(ones1[:, :], 1.0))
        vop(lambda v: v.memset(one11[:, :], 1.0))
        vop(lambda v: v.memset(ones64b[:, :], 1.0))
        vop(lambda v: v.memset(epsv[:, :], EPS))
        vop(lambda v: v.memset(zz[:, :], 0.0))
        ev_const = P.sig("v")

        for ev in ld:
            P.wait("t", ev)
        P.wait("t", ev_wkv)
        P.wait("t", ev_const)

        for mt in range(2):
            for dc in range(NDC):
                c0 = dc * 256 + mt * 128
                P.em("t", lambda t, c0=c0, mt=mt, dc=dc: t.matmul(
                    PS[3][:, 512 + mt * 128: 513 + mt * 128],
                    wbuf[:, c0:c0 + 128], onesb[:, :],
                    start=(dc == 0), stop=(dc == NDC - 1)))
        ev_cs = P.sig("t")
        P.wait("v", ev_cs)
        vop(lambda v: v.tensor_scalar_mul(csneg[0][:, :], PS[3][:, 512:513], -1.0))
        vop(lambda v: v.tensor_scalar_mul(csneg[1][:, :], PS[3][:, 640:641], -1.0))
        ev_csn = P.sig("v")

        P.wait("t", ev_csn)
        for dc in range(NDC):
            P.em("t", lambda t, dc=dc: t.matmul(
                PS[3][0:1, 0:4], ones128f[:, :], gwf[:, dc * 4:dc * 4 + 4],
                start=(dc == 0), stop=(dc == NDC - 1)))
        ev_cs4 = P.sig("t")
        P.wait("v", ev_cs4)
        vop(lambda v: v.tensor_scalar_mul(csr[:, :], PS[3][0:1, 0:4], -1.0))
        ev_csr = P.sig("v")
        P.wait("t", ev_csr)
        P.em("t", lambda t: t.matmul(PS[3][:, 4:8], ones1[:, :], csr[:, :],
                                     start=True, stop=True))
        ev_csb = P.sig("t")
        P.wait("v", ev_csb)
        vop(lambda v: v.tensor_copy(ncsb[:, :], PS[3][:, 4:8]))
        ev_ncsb = P.sig("v")

        state = {"sq_mm": None,
                 "rows_v": None,
                 "bc_v": None}

        def stats_from(src_bf, sq_dst):
            P.wait("v", state["sq_mm"])
            vop(lambda v: v.tensor_mul(sq_dst, src_bf, src_bf))
            ev_sq_v = P.sig("v")
            P.wait("t", state["rows_v"])
            for dc in range(NDC):
                P.em("t", lambda t, dc=dc: t.matmul(
                    PS[1][0:1, 0:512], onesb[:, :],
                    src_bf[:, dc * 512:dc * 512 + 512],
                    start=(dc == 0), stop=(dc == NDC - 1)))
            P.wait("t", ev_sq_v)
            for dc in range(NDC):
                P.em("t", lambda t, dc=dc: t.matmul(
                    PS[1][0:1, 512:1024], onesb[:, :],
                    sq_dst[:, dc * 512:dc * 512 + 512],
                    start=(dc == 0), stop=(dc == NDC - 1)))
            ev = P.sig("t")
            state["sq_mm"] = ev
            return ev

        def ln_rows(ev_stats):
            P.wait("v", ev_stats)
            vop(lambda v: v.tensor_scalar_mul(m_row, PS[1][0:1, 0:512], INV_D))
            vop(lambda v: v.tensor_mul(var_row, m_row, m_row))
            vop(lambda v: v.scalar_tensor_tensor(
                out=var_row, in0=PS[1][0:1, 512:1024], scalar=INV_D,
                in1=var_row, op0=ALU.mult, op1=ALU.subtract))
            ev_v1 = P.sig("v")
            state["rows_v"] = ev_v1
            P.wait("s", ev_v1)
            P.em("s", lambda s: s.activation(tmp[0:1, 0:512], var_row, AF.Ln,
                                             bias=epsv[0:1, :], scale=1.0))
            P.em("s", lambda s: s.activation(r_row, tmp[0:1, 0:512], AF.Exp,
                                             bias=zz[0:1, :], scale=-0.5))
            ev_r = P.sig("s")
            P.wait("v", ev_r)
            vop(lambda v: v.tensor_mul(mr_row, m_row, r_row))
            ev_mr = P.sig("v")
            P.wait("t", ev_mr)
            P.wait("t", state["bc_v"])
            P.em("t", lambda t: t.matmul(PS[2][:, 0:512], ones1[:, :], r_row,
                                         start=True, stop=True))
            P.em("t", lambda t: t.matmul(PS[2][:, 512:1024], ones1[:, :],
                                         mr_row, start=True, stop=True))
            ev_bc = P.sig("t")
            P.wait("v", ev_bc)
            vop(lambda v: v.tensor_copy(rbc[:, :], PS[2][:, 0:512]))
            vop(lambda v: v.tensor_copy(mrbc[:, :], PS[2][:, 512:1024]))
            ev = P.sig("v")
            state["bc_v"] = ev
            return ev

        ev_dma_gate = [None, None]
        ev_vn = ev_ncsb
        ev_kvT_last = None
        for ch in range(8):
            par = ch % 2
            P.wait("g", ev_dma_gate[par])
            ev_x = P.dma("g", xbK[par], xkv_in[:, ch, :], stream=f"gx{par}")
            P.wait("v", ev_x)
            P.wait("t", ev_x)
            ev_stats = stats_from(xbK[par], QYs[:, 0:16384])
            ev_rows = ln_rows(ev_stats)
            for mt in range(2):
                for dc in range(NDC):
                    c0 = dc * 256 + mt * 128
                    P.em("t", lambda t, c0=c0, mt=mt, dc=dc, par=par: t.matmul(
                        PS[0][:, mt * 512: mt * 512 + 512],
                        wbuf[:, c0:c0 + 128],
                        xbK[par][:, dc * 512: dc * 512 + 512],
                        start=(dc == 0), stop=(dc == NDC - 1)))
            ev_kv = P.sig("t")
            ev_kvT_last = ev_kv
            ev_dma_gate[par] = ev_kv
            P.wait("v", ev_kv)
            P.wait("v", ev_rows)
            P.wait("v", ev_csn)
            vop(lambda v: v.tensor_mul(tmp[:, :], PS[0][:, 0:512], rbc[:, :]))
            vop(lambda v, ch=ch: v.scalar_tensor_tensor(
                out=KT[:, ch * 512: ch * 512 + 512], in0=mrbc[:, :],
                scalar=csneg[0][:, :], in1=tmp[:, :],
                op0=ALU.mult, op1=ALU.add))
            vop(lambda v: v.tensor_mul(tmp[:, :], PS[0][:, 512:1024], rbc[:, :]))
            vop(lambda v: v.scalar_tensor_tensor(
                out=vtmp, in0=mrbc[:, :], scalar=csneg[1][:, :], in1=tmp[:, :],
                op0=ALU.mult, op1=ALU.add))
            ev_fold = P.sig("v")
            P.wait("t", ev_fold)
            P.wait("t", ev_vn)
            for kc in range(4):
                P.em("t", lambda t, kc=kc: t.matmul(
                    PS[3][:, kc * 128: kc * 128 + 128],
                    vtmp[:, kc * 128: kc * 128 + 128], eyeb[:, :],
                    start=True, stop=True))
            ev_vt = P.sig("t")
            P.wait("v", ev_vt)
            vop(lambda v, ch=ch: v.tensor_copy(
                Vn[:, ch * 512: ch * 512 + 512], PS[3][:, 0:512]))
            ev_vn = P.sig("v")

        ev_kv_done = ev_vn

        if PHASE == "kv":
            P.wait("g", ev_kv_done)
            e1 = P.dma("g", kt_out, KT)
            e2 = P.dma("g", vn_out, Vn)
            P.wait("g", e1)
            P.wait("g", e2)
            P.emit_all()
            return nc

        XLN = XAs
        qT = QYs
        Y = QYs
        attnT = XAs
        ev_store_prev = None
        wbuf_free = ev_kvT_last

        for g in range(1 if PHASE == "g0" else 2):
            P.wait("y", ev_store_prev)
            P.wait("y", ev_kv_done)
            ev_xr = P.dma("y", XRf[:, :], xg_in[g])
            P.wait("s", ev_xr)
            P.em("s", lambda s: s.copy(QYs[:, 0:16384], XRf[:, :]))
            ev_xbg = P.sig("s")
            P.wait("t", ev_xbg)
            P.wait("v", ev_xbg)
            ev_stats = stats_from(QYs[:, 0:16384], XAs[:, 0:16384])
            ev_rows = ln_rows(ev_stats)
            P.wait("v", ev_rows)
            for dc in range(NDC):
                vop(lambda v, dc=dc: v.tensor_mul(
                    tmp[:, :], XRf[:, dc * 512:dc * 512 + 512], rbc[:, :]))
                vop(lambda v, dc=dc: v.tensor_sub(
                    XLN[:, dc * 512:dc * 512 + 512], tmp[:, :], mrbc[:, :]))
            ev_ln1 = P.sig("v")

            ev_w = [None, None]
            P.wait("g", wbuf_free)
            ev_w[0] = P.dma("g", wbuf[:, 0:4096], wq_in[0], stream="w0")
            ev_w[1] = P.dma("g", wbuf[:, 4096:8192], wq_in[1], stream="w1")
            P.wait("t", ev_ln1)
            ev_qe = [None, None]
            for ot in range(NDC):
                par = ot % 2
                P.wait("t", ev_w[par])
                P.wait("t", ev_qe[par])
                for dc in range(NDC):
                    P.em("t", lambda t, par=par, dc=dc: t.matmul(
                        PS[0][:, par * 512: par * 512 + 512],
                        wbuf[:, par * 4096 + dc * 128:
                             par * 4096 + dc * 128 + 128],
                        XLN[:, dc * 512: dc * 512 + 512],
                        start=(dc == 0), stop=(dc == NDC - 1)))
                ev_mm = P.sig("t")
                wbuf_free = ev_mm
                if ot + 2 < NDC:
                    P.wait("g", ev_mm)
                    ev_w[par] = P.dma("g", wbuf[:, par * 4096: par * 4096 + 4096],
                                      wq_in[ot + 2], stream=f"w{par}")
                P.wait("s", ev_mm)
                P.em("s", lambda s, ot=ot, par=par: s.mul(
                    qT[:, ot * 512: ot * 512 + 512],
                    PS[0][:, par * 512: par * 512 + 512], SCALE))
                ev_qe[par] = P.sig("s")
            ev_qdone = ev_qe[(NDC - 1) % 2]
            if TAPS and g == 0:
                P.wait("g", ev_qdone)
                P.wait("g", ev_qe[0])
                P.dma("g", qt_o, qT[:, :])

            sc_par = [0]
            stg_par = [0]
            ev_sc_free = [None, None]
            ev_ot_free = [None, None]
            ev_exp_prev = [None, None]
            ev_pread = [None, None]
            ev_ptsread = [None, None]
            pend = []
            P.wait("t", state["rows_v"])

            def emit_trans_av(h, sg, nb, ev_diag, par_p):
                P.wait("t", ev_diag)
                nkc = nb * 4
                niter = (nb + 1) // 2
                first = True
                for it in range(niter):
                    kc0 = it * 8
                    kcn = min(8, nkc - kc0)
                    sp = it % 2
                    for i in range(kcn):
                        P.em("t", lambda t, i=i, kc=kc0 + i, par_p=par_p:
                             t.matmul(PS[2][:, i * 128: i * 128 + 128],
                                      pv[par_p][:, kc * 128: kc * 128 + 128],
                                      diag[:, par_p * 128: par_p * 128 + 128],
                                      start=True, stop=True))
                    ev_t1 = P.sig("t")
                    P.wait("s", ev_t1)
                    P.em("s", lambda s, sp=sp, w=kcn * 128: s.copy(
                        pts[sp][:, 0:w], PS[2][:, 0:w]))
                    ev_stg = P.sig("s")
                    P.wait("t", ev_stg)
                    if first:
                        P.wait("t", ev_ot_free[par_p])
                    for i in range(kcn):
                        P.em("t", lambda t, sp=sp, i=i, kc=kc0 + i, op=par_p,
                             st=(first and i == 0),
                             lst=(it == niter - 1 and i == kcn - 1):
                             t.matmul(PS[3][:, op * 512: op * 512 + 128],
                                      Vn[:, kc * 128: kc * 128 + 128],
                                      pts[sp][:, i * 128: i * 128 + 128],
                                      start=st, stop=lst))
                        first = False
                last_t = P.sig("t")
                P.wait("v", last_t)
                vop(lambda v, h=h, sg=sg, op=par_p: v.tensor_copy(
                    attnT[:, h * 512 + sg * 128: h * 512 + sg * 128 + 128],
                    PS[3][:, op * 512: op * 512 + 128]))
                ev_ot_free[par_p] = P.sig("v")
                ev_pread[par_p] = last_t

            for ui in range(H * 4):
                h, sg = ui // 4, ui % 4
                s_slot = 4 * g + sg
                nb = s_slot + 1
                par_p = ui % 2
                s1p = ui % 2
                P.wait("t", ev_qdone)
                P.wait("v", ev_exp_prev[s1p])
                for w0 in range(0, nb, 4):
                    wn = min(4, nb - w0)
                    P.wait("t", ev_sc_free[0])
                    for i in range(wn):
                        kb = w0 + i
                        P.em("t", lambda t, i=i, kb=kb, h=h, sg=sg: t.matmul(
                            PS[i // 2][:, (i % 2) * 512: (i % 2) * 512 + 512],
                            qT[:, h * 512 + sg * 128: h * 512 + sg * 128 + 128],
                            KT[:, kb * 512: kb * 512 + 512],
                            start=True, stop=True))
                    ev_sc = P.sig("t")
                    P.wait("v", ev_sc)
                    for pi in range((wn + 1) // 2):
                        W = min(1024, (wn - pi * 2) * 512)
                        vop(lambda v, pi=pi, W=W, w0=w0, h=h, s1p=s1p:
                            v.scalar_tensor_tensor(
                                out=s1v[s1p][:, (w0 + pi * 2) * 512:
                                             (w0 + pi * 2) * 512 + W],
                                in0=KPOS[:, (w0 + pi * 2) * 512:
                                          (w0 + pi * 2) * 512 + W],
                                scalar=SLOPES[h], in1=PS[pi][:, 0:W],
                                op0=ALU.mult, op1=ALU.add))
                    if w0 + 4 < nb:
                        ev_sc_free[0] = P.sig("v")
                vop(lambda v, nb=nb, s1p=s1p: v.tensor_add(
                    s1v[s1p][:, (nb - 1) * 512: nb * 512],
                    s1v[s1p][:, (nb - 1) * 512: nb * 512], dmask[:, :]))
                vop(lambda v, h=h, s_slot=s_slot, par_p=par_p:
                    v.tensor_scalar_mul(
                        ebias[par_p][:, :],
                        qpos[:, s_slot:s_slot + 1], -SLOPES[h]))
                ev_vu = P.sig("v")
                ev_sc_free[0] = ev_vu
                if len(pend) >= 2:
                    emit_trans_av(*pend.pop(0))
                P.wait("s", ev_vu)
                P.wait("s", ev_pread[par_p])
                P.em("s", lambda s, nb=nb, par_p=par_p, s1p=s1p: s.activation(
                    pv[par_p][:, 0:nb * 512], s1v[s1p][:, 0:nb * 512], AF.Exp,
                    bias=ebias[par_p][:, :],
                    accum_out=den[par_p][:, :]))
                ev_exp = P.sig("s")
                ev_exp_prev[s1p] = ev_exp
                P.wait("v", ev_exp)
                P.wait("v", ev_pread[par_p])
                vop(lambda v, par_p=par_p: v.reciprocal(
                    rden[par_p][:, :], den[par_p][:, :]))
                P.sig("v")
                vop(lambda v, par_p=par_p: v.tensor_scalar_mul(
                    diag[:, par_p * 128: par_p * 128 + 128], eyeb[:, :],
                    rden[par_p][:, :]))
                ev_diag = P.sig("v")
                pend.append((h, sg, nb, ev_diag, par_p))
            while pend:
                emit_trans_av(*pend.pop(0))

            P.wait("t", ev_sc_free[0])
            P.wait("t", ev_sc_free[1])
            P.wait("t", ev_ot_free[0])
            P.wait("t", ev_ot_free[1])

            if TAPS and g == 0:
                P.wait("g", ev_ot_free[0])
                P.wait("g", ev_ot_free[1])
                P.dma("g", at_o, attnT[:, :])
                P.wait("y", ev_ot_free[0])
                P.wait("y", ev_ot_free[1])
                P.dma("y", de_o[0], den[0][:, :])
                P.dma("y", de_o[1], den[1][:, :])
                P.dma("y", de_o[2], rden[0][:, :])
                P.dma("y", de_o[3], rden[1][:, :])
                P.dma("g", dg_o, diag[:, :])
                P.dma("g", p_o, pv[1][:, :])
            P.wait("y", ev_exp_prev[0])
            P.wait("y", ev_exp_prev[1])
            P.wait("y", ev_pread[0])
            P.wait("y", ev_pread[1])
            ev_xr2 = P.dma("y", XRf[:, :], xg_in[g])

            def proj_with_stats(w_in, dst, src, add_bias, extra_mm):
                nonlocal wbuf_free
                evw = [None, None]
                P.wait("g", wbuf_free)
                evw[0] = P.dma("g", wbuf[:, 0:4096], w_in[0], stream="w0")
                evw[1] = P.dma("g", wbuf[:, 4096:8192], w_in[1], stream="w1")
                ev_ev = [None, None]
                ev_ysq = None
                for ot in range(NDC):
                    par = ot % 2
                    P.wait("t", evw[par])
                    P.wait("t", ev_ev[par])
                    for dc in range(NDC):
                        P.em("t", lambda t, par=par, dc=dc: t.matmul(
                            PS[0][:, par * 512: par * 512 + 512],
                            wbuf[:, par * 4096 + dc * 128:
                                 par * 4096 + dc * 128 + 128],
                            src[:, dc * 512: dc * 512 + 512],
                            start=(dc == 0),
                            stop=(dc == NDC - 1) and not extra_mm))
                    if extra_mm:
                        for e in range(4):
                            r = RANKS[e]
                            P.em("t", lambda t, par=par, e=e, r=r, ot=ot:
                                 t.matmul(
                                     PS[0][:, par * 512: par * 512 + 512],
                                     btb[0:r, ot * 128: ot * 128 + 128],
                                     ue[0:r, e * 512: e * 512 + 512],
                                     start=False, stop=(e == 3)))
                    ev_mm = P.sig("t")
                    wbuf_free = ev_mm
                    if ot + 2 < NDC:
                        P.wait("g", ev_mm)
                        evw[par] = P.dma(
                            "g", wbuf[:, par * 4096: par * 4096 + 4096],
                            w_in[ot + 2], stream=f"w{par}")
                    P.wait("s", ev_mm)
                    if add_bias:
                        P.em("s", lambda s, ot=ot, par=par: s.add(
                            dst[:, ot * 512: ot * 512 + 512],
                            PS[0][:, par * 512: par * 512 + 512],
                            biast[:, ot:ot + 1]))
                    else:
                        P.em("s", lambda s, ot=ot, par=par: s.copy(
                            dst[:, ot * 512: ot * 512 + 512],
                            PS[0][:, par * 512: par * 512 + 512]))
                    ev_evac = P.sig("s")
                    ev_ev[par] = ev_evac
                    P.wait("v", ev_evac)
                    if ev_ysq is not None:
                        P.wait("v", ev_ysq)
                    vop(lambda v, ot=ot, sl=ot % 2: v.tensor_mul(
                        ysq[:, sl * 256: sl * 256 + 512]
                        if False else tmpc[:, (ot % 2) * 512:
                                           (ot % 2) * 512 + 512],
                        dst[:, ot * 512: ot * 512 + 512],
                        dst[:, ot * 512: ot * 512 + 512]))
                    ev_sqv = P.sig("v")
                    P.wait("t", ev_evac)
                    if ot == 0:
                        P.wait("t", state["rows_v"])
                    P.em("t", lambda t, ot=ot: t.matmul(
                        PS[1][0:1, 0:512], onesb[:, :],
                        dst[:, ot * 512: ot * 512 + 512],
                        start=(ot == 0), stop=(ot == NDC - 1)))
                    P.wait("t", ev_sqv)
                    P.em("t", lambda t, ot=ot, sl=ot % 2: t.matmul(
                        PS[1][0:1, 512:1024], onesb[:, :],
                        tmpc[:, sl * 512: sl * 512 + 512],
                        start=(ot == 0), stop=(ot == NDC - 1)))
                    if ot >= NDC - 2:
                        ev_ysq = P.sig("t")
                    else:
                        ev_ysq = ev_mm
                state["sq_mm"] = ev_ysq
                return ev_ysq

            ev_stats = proj_with_stats(wo_in, Y, attnT, False, False)
            ev_rows = ln_rows(ev_stats)
            P.wait("v", ev_rows)
            P.wait("v", ev_xr2)
            for dc in range(NDC):
                vop(lambda v, dc=dc: v.tensor_mul(
                    tmp[:, :], Y[:, dc * 512: dc * 512 + 512], rbc[:, :]))
                vop(lambda v: v.tensor_sub(tmp[:, :], tmp[:, :], mrbc[:, :]))
                vop(lambda v, dc=dc: v.tensor_add(
                    XRf[:, dc * 512: dc * 512 + 512],
                    XRf[:, dc * 512: dc * 512 + 512], tmp[:, :]))
            ev_res1 = P.sig("v")

            if TAPS and g == 0:
                P.wait("y", ev_res1)
                P.dma("y", xr_o, XRf[:, :])
            P.wait("s", ev_res1)
            P.em("s", lambda s: s.copy(QYs[:, 0:16384], XRf[:, :]))
            ev_xbg2 = P.sig("s")
            P.wait("t", ev_xbg2)
            P.wait("v", ev_xbg2)
            ev_stats = stats_from(QYs[:, 0:16384], XAs[:, 0:16384])
            ev_rows = ln_rows(ev_stats)
            P.wait("v", ev_rows)
            for dc in range(NDC):
                vop(lambda v, dc=dc: v.tensor_mul(
                    tmp[:, :], XRf[:, dc * 512: dc * 512 + 512], rbc[:, :]))
                vop(lambda v, dc=dc: v.tensor_sub(
                    XLN[:, dc * 512: dc * 512 + 512], tmp[:, :], mrbc[:, :]))
            ev_ln2 = P.sig("v")

            P.wait("t", ev_rows)
            ev_oht = None
            ev_r2c = None
            ev_oh = None
            ev_ohtc = None
            for tt in range(4):
                P.wait("t", ev_r2c)
                P.em("t", lambda t, tt=tt: t.matmul(
                    PS[2][:, 512:513], r_row[:, tt * 128: tt * 128 + 128],
                    one11[:, :], start=True, stop=True))
                P.em("t", lambda t, tt=tt: t.matmul(
                    PS[2][:, 513:514], mr_row[:, tt * 128: tt * 128 + 128],
                    one11[:, :], start=True, stop=True))
                ev_cols = P.sig("t")
                P.wait("v", ev_cols)
                vop(lambda v: v.tensor_copy(r2a[:, :], PS[2][:, 512:513]))
                vop(lambda v: v.tensor_copy(r2b[:, :], PS[2][:, 513:514]))
                ev_r2c = P.sig("v")
                P.wait("t", ev_oh)
                for dc in range(NDC):
                    P.em("t", lambda t, tt=tt, dc=dc: t.matmul(
                        PS[3][:, 0:4],
                        XRf[:, dc * 512 + tt * 128: dc * 512 + tt * 128 + 128],
                        gwf[:, dc * 4: dc * 4 + 4],
                        start=(dc == 0), stop=(dc == NDC - 1)))
                ev_g = P.sig("t")
                P.wait("v", ev_g)
                P.wait("v", ev_ncsb)
                P.wait("v", ev_r2c)
                vop(lambda v: v.tensor_scalar_mul(
                    Gs[:, :], PS[3][:, 0:4], r2a[:, :]))
                P.sig("v")
                vop(lambda v: v.scalar_tensor_tensor(
                    out=Gf[:, :], in0=ncsb[:, :], scalar=r2b[:, :],
                    in1=Gs[:, :], op0=ALU.mult, op1=ALU.add))
                P.sig("v")
                vop(lambda v: v.tensor_add(Gf[:, :], Gf[:, :], gbb[:, :]))
                P.sig("v")
                vop(lambda v: v.reduce_max(gmx[:, :], Gf[:, :], axis=AX))
                P.sig("v")
                vop(lambda v: v.tensor_scalar(
                    out=oh[:, :], in0=Gf[:, :], scalar1=gmx[:, :],
                    scalar2=None, op0=ALU.is_ge))
                P.sig("v")
                for e in range(4):
                    vop(lambda v, e=e: v.tensor_copy(
                        ohc4[e][:, :], oh[:, e:e + 1]))
                ev_oh = P.sig("v")
                P.wait("t", ev_oh)
                P.wait("t", ev_ohtc)
                for e in range(4):
                    P.em("t", lambda t, e=e: t.matmul(
                        PS[3][0:1, 512 + e * 128: 512 + e * 128 + 128],
                        ohc4[e][:, :], eyeb[:, :], start=True, stop=True))
                ev_oht = P.sig("t")
                P.wait("v", ev_oht)
                for e in range(4):
                    vop(lambda v, tt=tt, e=e: v.tensor_copy(
                        ohT[0:1, e * 512 + tt * 128: e * 512 + tt * 128 + 128],
                        PS[3][0:1, 512 + e * 128: 512 + e * 128 + 128]))
                ev_ohtc = P.sig("v")
            ev_ohT = ev_ohtc
            state["bc_v"] = ev_r2c
            if TAPS and g == 0:
                P.wait("v", ev_ohT)
                vop(lambda v: v.tensor_copy(tmp[:, 0:4], oh[:, :]))
                vop(lambda v: v.tensor_copy(tmp[:, 8:9], gmx[:, :]))
                for e in range(4):
                    vop(lambda v, e=e: v.tensor_copy(tmp[:, 4 + e:5 + e],
                                                     ohc4[e][:, :]))
                ev_tapv = P.sig("v")
                P.wait("y", ev_tapv)
                P.dma("y", ot_o, tmp[:, 0:16])
                P.wait("y", ev_oh)
                P.dma("y", g_o[:, 0:4], Gf[:, :])
                P.dma("y", g_o[:, 4:8], Gs[:, :])
                for e in range(4):
                    P.dma("g", oc_o[e], ohc4[e][:, :])
                P.dma("y", rc_o[0], r2a[:, :])
                P.dma("y", rc_o[1], r2b[:, :])

            P.wait("g", wbuf_free)
            ev_at = P.dma("g", wbuf[:, 0:NDC * 64], at_in, stream="w0")
            P.wait("t", ev_at)
            P.wait("t", ev_ln2)
            P.wait("t", ev_r2c)
            for dc in range(NDC):
                P.em("t", lambda t, dc=dc: t.matmul(
                    PS[2][0:64, 0:512], atb[:, dc * 64: dc * 64 + 64],
                    XLN[:, dc * 512: dc * 512 + 512],
                    start=(dc == 0), stop=(dc == NDC - 1)))
            ev_u = P.sig("t")
            wbuf_free = ev_u
            P.wait("v", ev_u)
            vop(lambda v: v.tensor_copy(u_sb[:, :], PS[2][0:64, 0:512]))
            ev_usb = P.sig("v")
            P.wait("t", ev_ohT)
            ev_ue = None
            for e in range(4):
                P.wait("t", ev_ue)
                P.em("t", lambda t, e=e: t.matmul(
                    PS[2][0:64, 512:1024], ones64b[:, :],
                    ohT[0:1, e * 512: e * 512 + 512],
                    start=True, stop=True))
                ev_ohb = P.sig("t")
                P.wait("v", ev_ohb)
                P.wait("v", ev_usb)
                vop(lambda v, e=e: v.tensor_mul(
                    ue[:, e * 512: e * 512 + 512], u_sb[:, :],
                    PS[2][0:64, 512:1024]))
                ev_ue = P.sig("v")
            P.wait("t", ev_ue)

            if TAPS and g == 0:
                P.wait("g", ev_ue)
                P.wait("g", ev_ohT)
                P.wait("g", ev_ln2)
                P.dma("g", oh_o, ohT[:, :])
                P.dma("g", u_o, u_sb[:, :])
                P.dma("g", ue_o, ue[:, :])
                P.dma("g", x2_o, XLN[:, :])
            ev_stats = proj_with_stats(wf_in, Y, XLN, True, True)
            if TAPS and g == 0:
                P.wait("g", ev_stats)
                P.dma("g", y_o, Y[:, :])
            ev_rows = ln_rows(ev_stats)
            P.wait("v", ev_rows)
            for dc in range(NDC):
                vop(lambda v, dc=dc: v.tensor_mul(
                    tmp[:, :], Y[:, dc * 512: dc * 512 + 512], rbc[:, :]))
                vop(lambda v: v.tensor_sub(tmp[:, :], tmp[:, :], mrbc[:, :]))
                vop(lambda v, dc=dc: v.tensor_add(
                    XRf[:, dc * 512: dc * 512 + 512],
                    XRf[:, dc * 512: dc * 512 + 512], tmp[:, :]))
            ev_res2 = P.sig("v")
            P.wait("y", ev_res2)
            ev_store_prev = P.dma("y", yt_out[g], XRf[:, :])
            ev_kv_done = None

        P.wait("y", ev_store_prev)
        P.wait("g", ev_store_prev)
        P.emit_all()
    return nc



def _prep(inputs):
    x = np.asarray(inputs["L0_attn_hidden_states"], np.float32)
    Wq = np.asarray(inputs["L0_attn_Wq"], np.float32)
    Wkv = np.asarray(inputs["L0_attn_Wkv"], np.float32)
    Wo = np.asarray(inputs["L0_attn_Wo"], np.float32)
    Wf = np.asarray(inputs["L0_ffn_W"], np.float32)
    fb = np.asarray(inputs["L0_ffn_b"], np.float32)
    A = np.asarray(inputs["L0_ffn_A"], np.float32)
    Bm = np.asarray(inputs["L0_ffn_B"], np.float32)
    gw = np.asarray(inputs["L0_ffn_gate_w"], np.float32)
    gb = np.asarray(inputs["L0_ffn_gate_b"], np.float32)

    def tile_w(W):
        return np.ascontiguousarray(
            W.reshape(32, 128, 32, 128).transpose(2, 1, 0, 3).reshape(
                32, 128, 32 * 128))

    shared = {
        "wq": tile_w(Wq),
        "wo": tile_w(Wo),
        "wf": tile_w(np.ascontiguousarray(Wf.T)),
        "wkv": np.ascontiguousarray(
            Wkv.reshape(32, 128, 256).transpose(1, 0, 2).reshape(128, 32 * 256)),
        "at": np.ascontiguousarray(
            A.T.reshape(32, 128, 64).transpose(1, 0, 2).reshape(128, 32 * 64)),
        "bt": np.ascontiguousarray(Bm.T),
        "gw": np.ascontiguousarray(
            gw.T.reshape(32, 128, 4).transpose(1, 0, 2).reshape(128, 32 * 4)),
        "fb": np.ascontiguousarray(fb.reshape(32, 128).T),
        "gb": np.ascontiguousarray(np.broadcast_to(gb, (128, 4))),
        "kpos": np.ascontiguousarray(
            np.broadcast_to(np.arange(4096, dtype=np.float32), (128, 4096))),
        "eye": np.eye(128, dtype=np.float32),
    }
    xT = [np.ascontiguousarray(x[b].T) for b in range(B)]
    xkv = [np.ascontiguousarray(
        xT[b].reshape(32, 128, 8, 512).transpose(1, 2, 0, 3).reshape(
            128, 8, 32 * 512)) for b in range(B)]

    in_maps = []
    tokmaps = []
    for c in range(8):
        b, j = c // 4, c % 4
        xg = np.empty((2, 128, 32 * 512), np.float32)
        toks = []
        for g in range(2):
            cols = np.concatenate(
                [np.arange(128) + (16 * g + 4 * sg + j) * 128
                 for sg in range(4)])
            toks.append(cols)
            xg[g] = xT[b][:, cols].reshape(32, 128, 512).transpose(
                1, 0, 2).reshape(128, 32 * 512)
        qi = np.arange(128, dtype=np.float32)[:, None]
        qpos = np.asarray(
            [[(4 * s + j) * 128 for s in range(8)]], np.float32) + qi
        kj = np.arange(512, dtype=np.float32)[None, :]
        dmask = np.where(kj > j * 128 + qi, np.float32(-1e30),
                         np.float32(0.0)).astype(np.float32)
        m = dict(shared)
        m["xg"] = xg
        m["xkv"] = xkv[b]
        m["qpos"] = np.ascontiguousarray(qpos)
        m["dmask"] = np.ascontiguousarray(dmask)
        in_maps.append(m)
        tokmaps.append(toks)
    return in_maps, tokmaps


def kernel(**inputs):
    in_maps, tokmaps = _prep(inputs)
    nc = build_program()
    res = run_bass_kernel_spmd(nc, in_maps, core_ids=list(range(8)))
    out = np.empty((B, S, D), np.float32)
    for c in range(8):
        b = c // 4
        yt = res.results[c]["yt"]
        for g in range(2):
            blk = yt[g].reshape(128, 32, 512).transpose(1, 0, 2).reshape(
                4096, 512)
            out[b, tokmaps[c][g], :] = blk.T
    return out


# version 18
# speedup vs baseline: 1.0053x; 1.0053x over previous
import contextlib
import numpy as np

import concourse.bass as bass
import concourse.mybir as mybir
from concourse.bass_utils import run_bass_kernel_spmd

F32 = mybir.dt.float32
BF16 = mybir.dt.bfloat16
AF = mybir.ActivationFunctionType
ALU = mybir.AluOpType
AX = mybir.AxisListType.X

B, S, D, H = 2, 4096, 4096, 32
NDC = 32
G = 512
SCALE = float(128) ** -0.5
SLOPES = [2.0 ** (-0.25 * (i + 1)) for i in range(H)]
RANKS = (8, 16, 32, 64)
INV_D = 1.0 / D
EPS = 1e-5
PHASE = "full"
TAPS = False


class Prog:
    ENGS = ("t", "v", "s", "y", "g")
    SEMS = ("t", "v", "s", "y", "g", "gx0", "gx1", "w0", "w1")

    def __init__(self, nc):
        self.nc = nc
        self.q = {e: [] for e in self.ENGS}
        self.cnt = {e: 0 for e in self.SEMS}
        self.lastw = {}
        self.sems = {}

    def handle(self, e):
        nc = self.nc
        return {"t": nc.tensor, "v": nc.vector, "s": nc.scalar,
                "y": nc.sync, "g": nc.gpsimd}[e]

    def em(self, e, fn):
        self.q[e].append(("op", fn))

    def sig(self, e):
        self.cnt[e] += 1
        v = self.cnt[e]
        self.q[e].append(("sig", v))
        return (e, v)

    def dma(self, e, out, in_, stream=None):
        st = stream or e
        self.cnt[st] += 16
        v = self.cnt[st]
        self.q[e].append(("dma", out, in_, v, st))
        return (st, v)

    def wait(self, e, ev):
        if ev is None:
            return
        pe, v = ev
        if pe == e and pe in ("t", "v", "s"):
            return
        if self.lastw.get((e, pe), 0) >= v:
            return
        self.lastw[(e, pe)] = v
        self.q[e].append(("wait", pe, v))

    def emit_all(self):
        nc = self.nc
        with nc.Block() as blk:
            decs = {"t": blk.tensor, "v": blk.vector, "s": blk.scalar,
                    "y": blk.sync, "g": blk.gpsimd}
            for e in self.ENGS:
                def section(eng, e=e):
                    h = self.handle(e)
                    for item in self.q[e]:
                        kind = item[0]
                        if kind == "op":
                            item[1](h)
                        elif kind == "sig":
                            h.drain().then_inc(self.sems[e], 1)
                        elif kind == "dma":
                            h.dma_start(out=item[1], in_=item[2]).then_inc(
                                self.sems[item[4]], 16)
                        elif kind == "wait":
                            h.wait_ge(self.sems[item[1]], item[2])
                decs[e](section)


def build_program():
    nc = bass.Bass("TRN2", target_bir_lowering=False)
    dt = nc.dram_tensor

    xg_in = dt("xg", [2, 128, NDC * G], F32, kind="ExternalInput").ap()
    xkv_in = dt("xkv", [128, 8, NDC * G], F32, kind="ExternalInput").ap()
    wq_in = dt("wq", [32, 128, NDC * 128], F32, kind="ExternalInput").ap()
    wo_in = dt("wo", [32, 128, NDC * 128], F32, kind="ExternalInput").ap()
    wf_in = dt("wf", [32, 128, NDC * 128], F32, kind="ExternalInput").ap()
    wkv_in = dt("wkv", [128, NDC * 256], F32, kind="ExternalInput").ap()
    at_in = dt("at", [128, NDC * 64], F32, kind="ExternalInput").ap()
    bt_in = dt("bt", [64, 4096], F32, kind="ExternalInput").ap()
    gw_in = dt("gw", [128, NDC * 4], F32, kind="ExternalInput").ap()
    bias_in = dt("fb", [128, 32], F32, kind="ExternalInput").ap()
    gb_in = dt("gb", [128, 4], F32, kind="ExternalInput").ap()
    kpos_in = dt("kpos", [128, 4096], F32, kind="ExternalInput").ap()
    eye_in = dt("eye", [128, 128], F32, kind="ExternalInput").ap()
    dmask_in = dt("dmask", [128, 512], F32, kind="ExternalInput").ap()
    qpos_in = dt("qpos", [128, 8], F32, kind="ExternalInput").ap()
    yt_out = dt("yt", [2, 128, NDC * G], F32, kind="ExternalOutput").ap()
    if PHASE == "kv":
        kt_out = dt("kt_o", [128, 4096], F32, kind="ExternalOutput").ap()
        vn_out = dt("vn_o", [128, 4096], F32, kind="ExternalOutput").ap()
    if TAPS:
        qt_o = dt("qt_o", [128, NDC * G], F32, kind="ExternalOutput").ap()
        de_o = dt("de_o", [4, 128, 1], F32, kind="ExternalOutput").ap()
        dg_o = dt("dg_o", [128, 256], F32, kind="ExternalOutput").ap()
        p_o = dt("p_o", [128, 4096], F32, kind="ExternalOutput").ap()
        oh_o = dt("oh_o", [1, 2048], F32, kind="ExternalOutput").ap()
        u_o = dt("u_o", [64, 512], F32, kind="ExternalOutput").ap()
        ue_o = dt("ue_o", [64, 2048], F32, kind="ExternalOutput").ap()
        y_o = dt("y_o", [128, NDC * G], F32, kind="ExternalOutput").ap()
        x2_o = dt("x2_o", [128, NDC * G], F32, kind="ExternalOutput").ap()
        g_o = dt("g_o", [128, 8], F32, kind="ExternalOutput").ap()
        rc_o = dt("rc_o", [2, 128, 1], F32, kind="ExternalOutput").ap()
        oc_o = dt("oc_o", [4, 128, 1], F32, kind="ExternalOutput").ap()
        ot_o = dt("ot_o", [128, 16], F32, kind="ExternalOutput").ap()
        at_o = dt("at_o", [128, NDC * G], F32, kind="ExternalOutput").ap()
        xr_o = dt("xr_o", [128, NDC * G], F32, kind="ExternalOutput").ap()

    P = Prog(nc)

    with contextlib.ExitStack() as ctx:
        E = ctx.enter_context
        POOL = 94656
        pool_h = E(nc.sbuf_tensor("pool", [128, POOL], BF16))
        pool = pool_h[:, :]
        cur = [0]

        def ab(cols, align=64):
            a = (cur[0] + align - 1) // align * align
            cur[0] = a + cols
            assert cur[0] <= POOL, f"pool overflow {cur[0]}"
            return pool[:, a:a + cols]

        def af(cols_f32, align=64):
            return ab(2 * cols_f32, align).bitcast(F32)

        KT = ab(4096)
        Vn = ab(4096)
        KPOS = af(4096)
        dmask = ab(512)
        eyeb = ab(128)
        qpos = af(8)
        wbuf = ab(8192)
        XRs = ab(32768)
        XAs = ab(16384)
        QYs = ab(16384)
        atb = wbuf[:, 0:NDC * 64]
        gwf = af(NDC * 4)
        biast = af(32)
        gbb = af(4)
        ncsb = af(4)
        tmpc = ab(1024)
        tmp = tmpc.bitcast(F32)
        rbc = af(512)
        mrbc = af(512)
        ysq = tmpc[:, 0:512]

        XRf = XRs.bitcast(F32)
        s1v = [XRs[:, 0:8192].bitcast(F32), XRs[:, 8192:16384].bitcast(F32)]
        pv = [XRs[:, 16384:20480], XRs[:, 20480:24576]]
        pts = [XRs[:, 24576:26624], XRs[:, 26624:28672]]
        xbK = [XRs[:, 0:16384], XRs[:, 16384:32768]]
        vtmp = XAs[:, 0:512]

        btb = E(nc.sbuf_tensor("btb", [64, 4096], BF16))
        u_sb = E(nc.sbuf_tensor("u_sb", [64, 512], BF16))
        ue = E(nc.sbuf_tensor("ue", [64, 4 * 512], BF16))
        ohT = E(nc.sbuf_tensor("ohT", [1, 4 * 512], BF16))
        rows = E(nc.sbuf_tensor("rows", [1, 2 * 512], F32))
        csr = E(nc.sbuf_tensor("csr", [1, 4], F32))
        diag = E(nc.sbuf_tensor("diag", [128, 256], BF16))
        den = [E(nc.sbuf_tensor(f"den{i}", [128, 1], F32)) for i in range(2)]
        rden = [E(nc.sbuf_tensor(f"rden{i}", [128, 1], F32)) for i in range(2)]
        ebias = [E(nc.sbuf_tensor(f"ebias{i}", [128, 1], F32)) for i in range(2)]
        Gs = E(nc.sbuf_tensor("Gs", [128, 4], F32))
        Gf = E(nc.sbuf_tensor("Gf", [128, 4], F32))
        gmx = E(nc.sbuf_tensor("gmx", [128, 1], F32))
        oh = E(nc.sbuf_tensor("oh", [128, 4], F32))
        ohc4 = [E(nc.sbuf_tensor(f"ohc{i}", [128, 1], BF16)) for i in range(4)]
        r2a = E(nc.sbuf_tensor("r2a", [128, 1], F32))
        r2b = E(nc.sbuf_tensor("r2b", [128, 1], F32))
        onesb = E(nc.sbuf_tensor("onesb", [128, 1], BF16))
        ones128f = E(nc.sbuf_tensor("ones128f", [128, 1], F32))
        ones1 = E(nc.sbuf_tensor("ones1", [1, 128], F32))
        one11 = E(nc.sbuf_tensor("one11", [1, 1], F32))
        ones64b = E(nc.sbuf_tensor("ones64b", [1, 64], BF16))
        epsv = E(nc.sbuf_tensor("epsv", [128, 1], F32))
        zz = E(nc.sbuf_tensor("zz", [128, 1], F32))
        csneg = [E(nc.sbuf_tensor(f"csneg{i}", [128, 1], F32)) for i in range(2)]

        PS = [E(nc.psum_tensor(f"ps{i}", [128, 1024], F32)) for i in range(4)]

        for e in Prog.SEMS:
            P.sems[e] = E(nc.semaphore(f"sem_{e}"))

        m_row = rows[0:1, 0:512]
        var_row = rows[0:1, 512:1024]
        r_row = rows[0:1, 512:1024]
        mr_row = rows[0:1, 0:512]

        def vop(fn):
            P.em("v", fn)

        ld = []
        ld.append(P.dma("g", eyeb[:, :], eye_in))
        ld.append(P.dma("g", btb[:, :], bt_in))
        ld.append(P.dma("y", KPOS[:, :], kpos_in))
        ld.append(P.dma("g", dmask[:, :], dmask_in))
        ld.append(P.dma("y", qpos[:, :], qpos_in))
        ld.append(P.dma("y", gwf[:, :], gw_in))
        ld.append(P.dma("y", biast[:, :], bias_in))
        ld.append(P.dma("y", gbb[:, :], gb_in))
        ev_wkv = P.dma("g", wbuf[:, 0:8192], wkv_in)

        vop(lambda v: v.memset(onesb[:, :], 1.0))
        vop(lambda v: v.memset(ones128f[:, :], 1.0))
        vop(lambda v: v.memset(ones1[:, :], 1.0))
        vop(lambda v: v.memset(one11[:, :], 1.0))
        vop(lambda v: v.memset(ones64b[:, :], 1.0))
        vop(lambda v: v.memset(epsv[:, :], EPS))
        vop(lambda v: v.memset(zz[:, :], 0.0))
        ev_const = P.sig("v")

        mx = {}
        for ev in ld + [ev_wkv]:
            mx[ev[0]] = max(mx.get(ev[0], 0), ev[1])
        for se, v in mx.items():
            P.wait("t", (se, v))
        P.wait("t", ev_const)

        for mt in range(2):
            for dc in range(NDC):
                c0 = dc * 256 + mt * 128
                P.em("t", lambda t, c0=c0, mt=mt, dc=dc: t.matmul(
                    PS[3][:, 512 + mt * 128: 513 + mt * 128],
                    wbuf[:, c0:c0 + 128], onesb[:, :],
                    start=(dc == 0), stop=(dc == NDC - 1)))
        ev_cs = P.sig("t")
        P.wait("v", ev_cs)
        vop(lambda v: v.tensor_scalar_mul(csneg[0][:, :], PS[3][:, 512:513], -1.0))
        vop(lambda v: v.tensor_scalar_mul(csneg[1][:, :], PS[3][:, 640:641], -1.0))
        ev_csn = P.sig("v")

        P.wait("t", ev_csn)
        for dc in range(NDC):
            P.em("t", lambda t, dc=dc: t.matmul(
                PS[3][0:1, 0:4], ones128f[:, :], gwf[:, dc * 4:dc * 4 + 4],
                start=(dc == 0), stop=(dc == NDC - 1)))
        ev_cs4 = P.sig("t")
        P.wait("v", ev_cs4)
        vop(lambda v: v.tensor_scalar_mul(csr[:, :], PS[3][0:1, 0:4], -1.0))
        ev_csr = P.sig("v")
        P.wait("t", ev_csr)
        P.em("t", lambda t: t.matmul(PS[3][:, 4:8], ones1[:, :], csr[:, :],
                                     start=True, stop=True))
        ev_csb = P.sig("t")
        P.wait("v", ev_csb)
        vop(lambda v: v.tensor_copy(ncsb[:, :], PS[3][:, 4:8]))
        ev_ncsb = P.sig("v")

        state = {"sq_mm": None,
                 "rows_v": None,
                 "bc_v": None}

        def stats_from(src_bf, sq_dst):
            P.wait("v", state["sq_mm"])
            vop(lambda v: v.tensor_mul(sq_dst, src_bf, src_bf))
            ev_sq_v = P.sig("v")
            P.wait("t", state["rows_v"])
            for dc in range(NDC):
                P.em("t", lambda t, dc=dc: t.matmul(
                    PS[1][0:1, 0:512], onesb[:, :],
                    src_bf[:, dc * 512:dc * 512 + 512],
                    start=(dc == 0), stop=(dc == NDC - 1)))
            P.wait("t", ev_sq_v)
            for dc in range(NDC):
                P.em("t", lambda t, dc=dc: t.matmul(
                    PS[1][0:1, 512:1024], onesb[:, :],
                    sq_dst[:, dc * 512:dc * 512 + 512],
                    start=(dc == 0), stop=(dc == NDC - 1)))
            ev = P.sig("t")
            state["sq_mm"] = ev
            return ev

        def ln_rows(ev_stats):
            P.wait("v", ev_stats)
            vop(lambda v: v.tensor_scalar_mul(m_row, PS[1][0:1, 0:512], INV_D))
            vop(lambda v: v.tensor_mul(var_row, m_row, m_row))
            vop(lambda v: v.scalar_tensor_tensor(
                out=var_row, in0=PS[1][0:1, 512:1024], scalar=INV_D,
                in1=var_row, op0=ALU.mult, op1=ALU.subtract))
            ev_v1 = P.sig("v")
            state["rows_v"] = ev_v1
            P.wait("s", ev_v1)
            P.em("s", lambda s: s.activation(tmp[0:1, 0:512], var_row, AF.Ln,
                                             bias=epsv[0:1, :], scale=1.0))
            P.em("s", lambda s: s.activation(r_row, tmp[0:1, 0:512], AF.Exp,
                                             bias=zz[0:1, :], scale=-0.5))
            ev_r = P.sig("s")
            P.wait("v", ev_r)
            vop(lambda v: v.tensor_mul(mr_row, m_row, r_row))
            ev_mr = P.sig("v")
            P.wait("t", ev_mr)
            P.wait("t", state["bc_v"])
            P.em("t", lambda t: t.matmul(PS[2][:, 0:512], ones1[:, :], r_row,
                                         start=True, stop=True))
            P.em("t", lambda t: t.matmul(PS[2][:, 512:1024], ones1[:, :],
                                         mr_row, start=True, stop=True))
            ev_bc = P.sig("t")
            P.wait("v", ev_bc)
            vop(lambda v: v.tensor_copy(rbc[:, :], PS[2][:, 0:512]))
            vop(lambda v: v.tensor_copy(mrbc[:, :], PS[2][:, 512:1024]))
            ev = P.sig("v")
            state["bc_v"] = ev
            return ev

        ev_dma_gate = [None, None]
        ev_vn = ev_ncsb
        ev_kvT_last = None
        for ch in range(8):
            par = ch % 2
            P.wait("g", ev_dma_gate[par])
            ev_x = P.dma("g", xbK[par], xkv_in[:, ch, :], stream=f"gx{par}")
            P.wait("v", ev_x)
            P.wait("t", ev_x)
            ev_stats = stats_from(xbK[par], QYs[:, 0:16384])
            ev_rows = ln_rows(ev_stats)
            for mt in range(2):
                for dc in range(NDC):
                    c0 = dc * 256 + mt * 128
                    P.em("t", lambda t, c0=c0, mt=mt, dc=dc, par=par: t.matmul(
                        PS[0][:, mt * 512: mt * 512 + 512],
                        wbuf[:, c0:c0 + 128],
                        xbK[par][:, dc * 512: dc * 512 + 512],
                        start=(dc == 0), stop=(dc == NDC - 1)))
            ev_kv = P.sig("t")
            ev_kvT_last = ev_kv
            ev_dma_gate[par] = ev_kv
            P.wait("v", ev_kv)
            P.wait("v", ev_rows)
            P.wait("v", ev_csn)
            vop(lambda v: v.tensor_mul(tmp[:, :], PS[0][:, 0:512], rbc[:, :]))
            vop(lambda v, ch=ch: v.scalar_tensor_tensor(
                out=KT[:, ch * 512: ch * 512 + 512], in0=mrbc[:, :],
                scalar=csneg[0][:, :], in1=tmp[:, :],
                op0=ALU.mult, op1=ALU.add))
            vop(lambda v: v.tensor_mul(tmp[:, :], PS[0][:, 512:1024], rbc[:, :]))
            vop(lambda v: v.scalar_tensor_tensor(
                out=vtmp, in0=mrbc[:, :], scalar=csneg[1][:, :], in1=tmp[:, :],
                op0=ALU.mult, op1=ALU.add))
            ev_fold = P.sig("v")
            P.wait("t", ev_fold)
            P.wait("t", ev_vn)
            for kc in range(4):
                P.em("t", lambda t, kc=kc: t.matmul(
                    PS[3][:, kc * 128: kc * 128 + 128],
                    vtmp[:, kc * 128: kc * 128 + 128], eyeb[:, :],
                    start=True, stop=True))
            ev_vt = P.sig("t")
            P.wait("v", ev_vt)
            vop(lambda v, ch=ch: v.tensor_copy(
                Vn[:, ch * 512: ch * 512 + 512], PS[3][:, 0:512]))
            ev_vn = P.sig("v")

        ev_kv_done = ev_vn

        if PHASE == "kv":
            P.wait("g", ev_kv_done)
            e1 = P.dma("g", kt_out, KT)
            e2 = P.dma("g", vn_out, Vn)
            P.wait("g", e1)
            P.wait("g", e2)
            P.emit_all()
            return nc

        XLN = XAs
        qT = QYs
        Y = QYs
        attnT = XAs
        ev_store_prev = None
        wbuf_free = ev_kvT_last

        for g in range(1 if PHASE == "g0" else 2):
            P.wait("y", ev_store_prev)
            P.wait("y", ev_kv_done)
            ev_xr = P.dma("y", XRf[:, :], xg_in[g])
            P.wait("s", ev_xr)
            P.em("s", lambda s: s.copy(QYs[:, 0:16384], XRf[:, :]))
            ev_xbg = P.sig("s")
            P.wait("t", ev_xbg)
            P.wait("v", ev_xbg)
            ev_stats = stats_from(QYs[:, 0:16384], XAs[:, 0:16384])
            ev_rows = ln_rows(ev_stats)
            P.wait("v", ev_rows)
            for dc in range(NDC):
                vop(lambda v, dc=dc: v.tensor_mul(
                    tmp[:, :], XRf[:, dc * 512:dc * 512 + 512], rbc[:, :]))
                vop(lambda v, dc=dc: v.tensor_sub(
                    XLN[:, dc * 512:dc * 512 + 512], tmp[:, :], mrbc[:, :]))
            ev_ln1 = P.sig("v")

            ev_w = [None, None]
            P.wait("g", wbuf_free)
            ev_w[0] = P.dma("g", wbuf[:, 0:4096], wq_in[0], stream="w0")
            ev_w[1] = P.dma("g", wbuf[:, 4096:8192], wq_in[1], stream="w1")
            P.wait("t", ev_ln1)
            ev_qe = [None, None]
            for ot in range(NDC):
                par = ot % 2
                P.wait("t", ev_w[par])
                P.wait("t", ev_qe[par])
                for dc in range(NDC):
                    P.em("t", lambda t, par=par, dc=dc: t.matmul(
                        PS[0][:, par * 512: par * 512 + 512],
                        wbuf[:, par * 4096 + dc * 128:
                             par * 4096 + dc * 128 + 128],
                        XLN[:, dc * 512: dc * 512 + 512],
                        start=(dc == 0), stop=(dc == NDC - 1)))
                ev_mm = P.sig("t")
                wbuf_free = ev_mm
                if ot + 2 < NDC:
                    P.wait("g", ev_mm)
                    ev_w[par] = P.dma("g", wbuf[:, par * 4096: par * 4096 + 4096],
                                      wq_in[ot + 2], stream=f"w{par}")
                P.wait("s", ev_mm)
                P.em("s", lambda s, ot=ot, par=par: s.mul(
                    qT[:, ot * 512: ot * 512 + 512],
                    PS[0][:, par * 512: par * 512 + 512], SCALE))
                ev_qe[par] = P.sig("s")
            ev_qdone = ev_qe[(NDC - 1) % 2]
            if TAPS and g == 0:
                P.wait("g", ev_qdone)
                P.wait("g", ev_qe[0])
                P.dma("g", qt_o, qT[:, :])

            sc_par = [0]
            stg_par = [0]
            ev_sc_free = [None, None]
            ev_ot_free = [None, None]
            ev_exp_prev = [None, None]
            ev_pread = [None, None]
            ev_ptsread = [None, None]
            pend = []
            P.wait("t", state["rows_v"])

            def emit_trans_av(h, sg, nb, ev_diag, par_p):
                P.wait("t", ev_diag)
                nkc = nb * 4
                niter = (nb + 1) // 2
                first = True
                for it in range(niter):
                    kc0 = it * 8
                    kcn = min(8, nkc - kc0)
                    sp = it % 2
                    for i in range(kcn):
                        P.em("t", lambda t, i=i, kc=kc0 + i, par_p=par_p:
                             t.matmul(PS[2][:, i * 128: i * 128 + 128],
                                      pv[par_p][:, kc * 128: kc * 128 + 128],
                                      diag[:, par_p * 128: par_p * 128 + 128],
                                      start=True, stop=True))
                    ev_t1 = P.sig("t")
                    P.wait("s", ev_t1)
                    P.em("s", lambda s, sp=sp, w=kcn * 128: s.copy(
                        pts[sp][:, 0:w], PS[2][:, 0:w]))
                    ev_stg = P.sig("s")
                    P.wait("t", ev_stg)
                    if first:
                        P.wait("t", ev_ot_free[par_p])
                    for i in range(kcn):
                        P.em("t", lambda t, sp=sp, i=i, kc=kc0 + i, op=par_p,
                             st=(first and i == 0),
                             lst=(it == niter - 1 and i == kcn - 1):
                             t.matmul(PS[3][:, op * 512: op * 512 + 128],
                                      Vn[:, kc * 128: kc * 128 + 128],
                                      pts[sp][:, i * 128: i * 128 + 128],
                                      start=st, stop=lst))
                        first = False
                last_t = P.sig("t")
                P.wait("v", last_t)
                vop(lambda v, h=h, sg=sg, op=par_p: v.tensor_copy(
                    attnT[:, h * 512 + sg * 128: h * 512 + sg * 128 + 128],
                    PS[3][:, op * 512: op * 512 + 128]))
                ev_ot_free[par_p] = P.sig("v")
                ev_pread[par_p] = last_t

            for ui in range(H * 4):
                h, sg = ui // 4, ui % 4
                s_slot = 4 * g + sg
                nb = s_slot + 1
                par_p = ui % 2
                s1p = ui % 2
                P.wait("t", ev_qdone)
                P.wait("v", ev_exp_prev[s1p])
                for w0 in range(0, nb, 4):
                    wn = min(4, nb - w0)
                    P.wait("t", ev_sc_free[0])
                    for i in range(wn):
                        kb = w0 + i
                        P.em("t", lambda t, i=i, kb=kb, h=h, sg=sg: t.matmul(
                            PS[i // 2][:, (i % 2) * 512: (i % 2) * 512 + 512],
                            qT[:, h * 512 + sg * 128: h * 512 + sg * 128 + 128],
                            KT[:, kb * 512: kb * 512 + 512],
                            start=True, stop=True))
                    ev_sc = P.sig("t")
                    P.wait("v", ev_sc)
                    for pi in range((wn + 1) // 2):
                        W = min(1024, (wn - pi * 2) * 512)
                        vop(lambda v, pi=pi, W=W, w0=w0, h=h, s1p=s1p:
                            v.scalar_tensor_tensor(
                                out=s1v[s1p][:, (w0 + pi * 2) * 512:
                                             (w0 + pi * 2) * 512 + W],
                                in0=KPOS[:, (w0 + pi * 2) * 512:
                                          (w0 + pi * 2) * 512 + W],
                                scalar=SLOPES[h], in1=PS[pi][:, 0:W],
                                op0=ALU.mult, op1=ALU.add))
                    if w0 + 4 < nb:
                        ev_sc_free[0] = P.sig("v")
                vop(lambda v, nb=nb, s1p=s1p: v.tensor_add(
                    s1v[s1p][:, (nb - 1) * 512: nb * 512],
                    s1v[s1p][:, (nb - 1) * 512: nb * 512], dmask[:, :]))
                vop(lambda v, h=h, s_slot=s_slot, par_p=par_p:
                    v.tensor_scalar_mul(
                        ebias[par_p][:, :],
                        qpos[:, s_slot:s_slot + 1], -SLOPES[h]))
                ev_vu = P.sig("v")
                ev_sc_free[0] = ev_vu
                if len(pend) >= 2:
                    emit_trans_av(*pend.pop(0))
                P.wait("s", ev_vu)
                P.wait("s", ev_pread[par_p])
                P.em("s", lambda s, nb=nb, par_p=par_p, s1p=s1p: s.activation(
                    pv[par_p][:, 0:nb * 512], s1v[s1p][:, 0:nb * 512], AF.Exp,
                    bias=ebias[par_p][:, :],
                    accum_out=den[par_p][:, :]))
                ev_exp = P.sig("s")
                ev_exp_prev[s1p] = ev_exp
                P.wait("v", ev_exp)
                P.wait("v", ev_pread[par_p])
                vop(lambda v, par_p=par_p: v.reciprocal(
                    rden[par_p][:, :], den[par_p][:, :]))
                P.sig("v")
                vop(lambda v, par_p=par_p: v.tensor_scalar_mul(
                    diag[:, par_p * 128: par_p * 128 + 128], eyeb[:, :],
                    rden[par_p][:, :]))
                ev_diag = P.sig("v")
                pend.append((h, sg, nb, ev_diag, par_p))
            while pend:
                emit_trans_av(*pend.pop(0))

            P.wait("t", ev_sc_free[0])
            P.wait("t", ev_sc_free[1])
            P.wait("t", ev_ot_free[0])
            P.wait("t", ev_ot_free[1])

            if TAPS and g == 0:
                P.wait("g", ev_ot_free[0])
                P.wait("g", ev_ot_free[1])
                P.dma("g", at_o, attnT[:, :])
                P.wait("y", ev_ot_free[0])
                P.wait("y", ev_ot_free[1])
                P.dma("y", de_o[0], den[0][:, :])
                P.dma("y", de_o[1], den[1][:, :])
                P.dma("y", de_o[2], rden[0][:, :])
                P.dma("y", de_o[3], rden[1][:, :])
                P.dma("g", dg_o, diag[:, :])
                P.dma("g", p_o, pv[1][:, :])
            P.wait("y", ev_exp_prev[0])
            P.wait("y", ev_exp_prev[1])
            P.wait("y", ev_pread[0])
            P.wait("y", ev_pread[1])
            ev_xr2 = P.dma("y", XRf[:, :], xg_in[g])

            def proj_with_stats(w_in, dst, src, add_bias, extra_mm):
                nonlocal wbuf_free
                evw = [None, None]
                P.wait("g", wbuf_free)
                evw[0] = P.dma("g", wbuf[:, 0:4096], w_in[0], stream="w0")
                evw[1] = P.dma("g", wbuf[:, 4096:8192], w_in[1], stream="w1")
                ev_ev = [None, None]
                ev_ysq = None
                for ot in range(NDC):
                    par = ot % 2
                    P.wait("t", evw[par])
                    P.wait("t", ev_ev[par])
                    for dc in range(NDC):
                        P.em("t", lambda t, par=par, dc=dc: t.matmul(
                            PS[0][:, par * 512: par * 512 + 512],
                            wbuf[:, par * 4096 + dc * 128:
                                 par * 4096 + dc * 128 + 128],
                            src[:, dc * 512: dc * 512 + 512],
                            start=(dc == 0),
                            stop=(dc == NDC - 1) and not extra_mm))
                    if extra_mm:
                        for e in range(4):
                            r = RANKS[e]
                            P.em("t", lambda t, par=par, e=e, r=r, ot=ot:
                                 t.matmul(
                                     PS[0][:, par * 512: par * 512 + 512],
                                     btb[0:r, ot * 128: ot * 128 + 128],
                                     ue[0:r, e * 512: e * 512 + 512],
                                     start=False, stop=(e == 3)))
                    ev_mm = P.sig("t")
                    wbuf_free = ev_mm
                    if ot + 2 < NDC:
                        P.wait("g", ev_mm)
                        evw[par] = P.dma(
                            "g", wbuf[:, par * 4096: par * 4096 + 4096],
                            w_in[ot + 2], stream=f"w{par}")
                    P.wait("s", ev_mm)
                    if add_bias:
                        P.em("s", lambda s, ot=ot, par=par: s.add(
                            dst[:, ot * 512: ot * 512 + 512],
                            PS[0][:, par * 512: par * 512 + 512],
                            biast[:, ot:ot + 1]))
                    else:
                        P.em("s", lambda s, ot=ot, par=par: s.copy(
                            dst[:, ot * 512: ot * 512 + 512],
                            PS[0][:, par * 512: par * 512 + 512]))
                    ev_evac = P.sig("s")
                    ev_ev[par] = ev_evac
                    P.wait("v", ev_evac)
                    if ev_ysq is not None:
                        P.wait("v", ev_ysq)
                    vop(lambda v, ot=ot, sl=ot % 2: v.tensor_mul(
                        ysq[:, sl * 256: sl * 256 + 512]
                        if False else tmpc[:, (ot % 2) * 512:
                                           (ot % 2) * 512 + 512],
                        dst[:, ot * 512: ot * 512 + 512],
                        dst[:, ot * 512: ot * 512 + 512]))
                    ev_sqv = P.sig("v")
                    P.wait("t", ev_evac)
                    if ot == 0:
                        P.wait("t", state["rows_v"])
                    P.em("t", lambda t, ot=ot: t.matmul(
                        PS[1][0:1, 0:512], onesb[:, :],
                        dst[:, ot * 512: ot * 512 + 512],
                        start=(ot == 0), stop=(ot == NDC - 1)))
                    P.wait("t", ev_sqv)
                    P.em("t", lambda t, ot=ot, sl=ot % 2: t.matmul(
                        PS[1][0:1, 512:1024], onesb[:, :],
                        tmpc[:, sl * 512: sl * 512 + 512],
                        start=(ot == 0), stop=(ot == NDC - 1)))
                    if ot >= NDC - 2:
                        ev_ysq = P.sig("t")
                    else:
                        ev_ysq = ev_mm
                state["sq_mm"] = ev_ysq
                return ev_ysq

            ev_stats = proj_with_stats(wo_in, Y, attnT, False, False)
            ev_rows = ln_rows(ev_stats)
            P.wait("v", ev_rows)
            P.wait("v", ev_xr2)
            for dc in range(NDC):
                vop(lambda v, dc=dc: v.tensor_mul(
                    tmp[:, :], Y[:, dc * 512: dc * 512 + 512], rbc[:, :]))
                vop(lambda v: v.tensor_sub(tmp[:, :], tmp[:, :], mrbc[:, :]))
                vop(lambda v, dc=dc: v.tensor_add(
                    XRf[:, dc * 512: dc * 512 + 512],
                    XRf[:, dc * 512: dc * 512 + 512], tmp[:, :]))
            ev_res1 = P.sig("v")

            if TAPS and g == 0:
                P.wait("y", ev_res1)
                P.dma("y", xr_o, XRf[:, :])
            P.wait("s", ev_res1)
            P.em("s", lambda s: s.copy(QYs[:, 0:16384], XRf[:, :]))
            ev_xbg2 = P.sig("s")
            P.wait("t", ev_xbg2)
            P.wait("v", ev_xbg2)
            ev_stats = stats_from(QYs[:, 0:16384], XAs[:, 0:16384])
            ev_rows = ln_rows(ev_stats)
            P.wait("v", ev_rows)
            for dc in range(NDC):
                vop(lambda v, dc=dc: v.tensor_mul(
                    tmp[:, :], XRf[:, dc * 512: dc * 512 + 512], rbc[:, :]))
                vop(lambda v, dc=dc: v.tensor_sub(
                    XLN[:, dc * 512: dc * 512 + 512], tmp[:, :], mrbc[:, :]))
            ev_ln2 = P.sig("v")

            P.wait("t", ev_rows)
            ev_oht = None
            ev_r2c = None
            ev_oh = None
            ev_ohtc = None
            for tt in range(4):
                P.wait("t", ev_r2c)
                P.em("t", lambda t, tt=tt: t.matmul(
                    PS[2][:, 512:513], r_row[:, tt * 128: tt * 128 + 128],
                    one11[:, :], start=True, stop=True))
                P.em("t", lambda t, tt=tt: t.matmul(
                    PS[2][:, 513:514], mr_row[:, tt * 128: tt * 128 + 128],
                    one11[:, :], start=True, stop=True))
                ev_cols = P.sig("t")
                P.wait("v", ev_cols)
                vop(lambda v: v.tensor_copy(r2a[:, :], PS[2][:, 512:513]))
                vop(lambda v: v.tensor_copy(r2b[:, :], PS[2][:, 513:514]))
                ev_r2c = P.sig("v")
                P.wait("t", ev_oh)
                for dc in range(NDC):
                    P.em("t", lambda t, tt=tt, dc=dc: t.matmul(
                        PS[3][:, 0:4],
                        XRf[:, dc * 512 + tt * 128: dc * 512 + tt * 128 + 128],
                        gwf[:, dc * 4: dc * 4 + 4],
                        start=(dc == 0), stop=(dc == NDC - 1)))
                ev_g = P.sig("t")
                P.wait("v", ev_g)
                P.wait("v", ev_ncsb)
                P.wait("v", ev_r2c)
                vop(lambda v: v.tensor_scalar_mul(
                    Gs[:, :], PS[3][:, 0:4], r2a[:, :]))
                P.sig("v")
                vop(lambda v: v.scalar_tensor_tensor(
                    out=Gf[:, :], in0=ncsb[:, :], scalar=r2b[:, :],
                    in1=Gs[:, :], op0=ALU.mult, op1=ALU.add))
                P.sig("v")
                vop(lambda v: v.tensor_add(Gf[:, :], Gf[:, :], gbb[:, :]))
                P.sig("v")
                vop(lambda v: v.reduce_max(gmx[:, :], Gf[:, :], axis=AX))
                P.sig("v")
                vop(lambda v: v.tensor_scalar(
                    out=oh[:, :], in0=Gf[:, :], scalar1=gmx[:, :],
                    scalar2=None, op0=ALU.is_ge))
                P.sig("v")
                for e in range(4):
                    vop(lambda v, e=e: v.tensor_copy(
                        ohc4[e][:, :], oh[:, e:e + 1]))
                ev_oh = P.sig("v")
                P.wait("t", ev_oh)
                P.wait("t", ev_ohtc)
                for e in range(4):
                    P.em("t", lambda t, e=e: t.matmul(
                        PS[3][0:1, 512 + e * 128: 512 + e * 128 + 128],
                        ohc4[e][:, :], eyeb[:, :], start=True, stop=True))
                ev_oht = P.sig("t")
                P.wait("v", ev_oht)
                for e in range(4):
                    vop(lambda v, tt=tt, e=e: v.tensor_copy(
                        ohT[0:1, e * 512 + tt * 128: e * 512 + tt * 128 + 128],
                        PS[3][0:1, 512 + e * 128: 512 + e * 128 + 128]))
                ev_ohtc = P.sig("v")
            ev_ohT = ev_ohtc
            state["bc_v"] = ev_r2c
            if TAPS and g == 0:
                P.wait("v", ev_ohT)
                vop(lambda v: v.tensor_copy(tmp[:, 0:4], oh[:, :]))
                vop(lambda v: v.tensor_copy(tmp[:, 8:9], gmx[:, :]))
                for e in range(4):
                    vop(lambda v, e=e: v.tensor_copy(tmp[:, 4 + e:5 + e],
                                                     ohc4[e][:, :]))
                ev_tapv = P.sig("v")
                P.wait("y", ev_tapv)
                P.dma("y", ot_o, tmp[:, 0:16])
                P.wait("y", ev_oh)
                P.dma("y", g_o[:, 0:4], Gf[:, :])
                P.dma("y", g_o[:, 4:8], Gs[:, :])
                for e in range(4):
                    P.dma("g", oc_o[e], ohc4[e][:, :])
                P.dma("y", rc_o[0], r2a[:, :])
                P.dma("y", rc_o[1], r2b[:, :])

            P.wait("g", wbuf_free)
            ev_at = P.dma("g", wbuf[:, 0:NDC * 64], at_in, stream="w0")
            P.wait("t", ev_at)
            P.wait("t", ev_ln2)
            P.wait("t", ev_r2c)
            for dc in range(NDC):
                P.em("t", lambda t, dc=dc: t.matmul(
                    PS[2][0:64, 0:512], atb[:, dc * 64: dc * 64 + 64],
                    XLN[:, dc * 512: dc * 512 + 512],
                    start=(dc == 0), stop=(dc == NDC - 1)))
            ev_u = P.sig("t")
            wbuf_free = ev_u
            P.wait("v", ev_u)
            vop(lambda v: v.tensor_copy(u_sb[:, :], PS[2][0:64, 0:512]))
            ev_usb = P.sig("v")
            P.wait("t", ev_ohT)
            ev_ue = None
            for e in range(4):
                P.wait("t", ev_ue)
                P.em("t", lambda t, e=e: t.matmul(
                    PS[2][0:64, 512:1024], ones64b[:, :],
                    ohT[0:1, e * 512: e * 512 + 512],
                    start=True, stop=True))
                ev_ohb = P.sig("t")
                P.wait("v", ev_ohb)
                P.wait("v", ev_usb)
                vop(lambda v, e=e: v.tensor_mul(
                    ue[:, e * 512: e * 512 + 512], u_sb[:, :],
                    PS[2][0:64, 512:1024]))
                ev_ue = P.sig("v")
            P.wait("t", ev_ue)

            if TAPS and g == 0:
                P.wait("g", ev_ue)
                P.wait("g", ev_ohT)
                P.wait("g", ev_ln2)
                P.dma("g", oh_o, ohT[:, :])
                P.dma("g", u_o, u_sb[:, :])
                P.dma("g", ue_o, ue[:, :])
                P.dma("g", x2_o, XLN[:, :])
            ev_stats = proj_with_stats(wf_in, Y, XLN, True, True)
            if TAPS and g == 0:
                P.wait("g", ev_stats)
                P.dma("g", y_o, Y[:, :])
            ev_rows = ln_rows(ev_stats)
            P.wait("v", ev_rows)
            for dc in range(NDC):
                vop(lambda v, dc=dc: v.tensor_mul(
                    tmp[:, :], Y[:, dc * 512: dc * 512 + 512], rbc[:, :]))
                vop(lambda v: v.tensor_sub(tmp[:, :], tmp[:, :], mrbc[:, :]))
                vop(lambda v, dc=dc: v.tensor_add(
                    XRf[:, dc * 512: dc * 512 + 512],
                    XRf[:, dc * 512: dc * 512 + 512], tmp[:, :]))
            ev_res2 = P.sig("v")
            P.wait("y", ev_res2)
            ev_store_prev = P.dma("y", yt_out[g], XRf[:, :])
            ev_kv_done = None

        P.wait("y", ev_store_prev)
        P.wait("g", ev_store_prev)
        P.emit_all()
    return nc



def _prep(inputs):
    x = np.asarray(inputs["L0_attn_hidden_states"], np.float32)
    Wq = np.asarray(inputs["L0_attn_Wq"], np.float32)
    Wkv = np.asarray(inputs["L0_attn_Wkv"], np.float32)
    Wo = np.asarray(inputs["L0_attn_Wo"], np.float32)
    Wf = np.asarray(inputs["L0_ffn_W"], np.float32)
    fb = np.asarray(inputs["L0_ffn_b"], np.float32)
    A = np.asarray(inputs["L0_ffn_A"], np.float32)
    Bm = np.asarray(inputs["L0_ffn_B"], np.float32)
    gw = np.asarray(inputs["L0_ffn_gate_w"], np.float32)
    gb = np.asarray(inputs["L0_ffn_gate_b"], np.float32)

    def tile_w(W):
        return np.ascontiguousarray(
            W.reshape(32, 128, 32, 128).transpose(2, 1, 0, 3).reshape(
                32, 128, 32 * 128))

    shared = {
        "wq": tile_w(Wq),
        "wo": tile_w(Wo),
        "wf": tile_w(np.ascontiguousarray(Wf.T)),
        "wkv": np.ascontiguousarray(
            Wkv.reshape(32, 128, 256).transpose(1, 0, 2).reshape(128, 32 * 256)),
        "at": np.ascontiguousarray(
            A.T.reshape(32, 128, 64).transpose(1, 0, 2).reshape(128, 32 * 64)),
        "bt": np.ascontiguousarray(Bm.T),
        "gw": np.ascontiguousarray(
            gw.T.reshape(32, 128, 4).transpose(1, 0, 2).reshape(128, 32 * 4)),
        "fb": np.ascontiguousarray(fb.reshape(32, 128).T),
        "gb": np.ascontiguousarray(np.broadcast_to(gb, (128, 4))),
        "kpos": np.ascontiguousarray(
            np.broadcast_to(np.arange(4096, dtype=np.float32), (128, 4096))),
        "eye": np.eye(128, dtype=np.float32),
    }
    xT = [np.ascontiguousarray(x[b].T) for b in range(B)]
    xkv = [np.ascontiguousarray(
        xT[b].reshape(32, 128, 8, 512).transpose(1, 2, 0, 3).reshape(
            128, 8, 32 * 512)) for b in range(B)]

    in_maps = []
    tokmaps = []
    for c in range(8):
        b, j = c // 4, c % 4
        xg = np.empty((2, 128, 32 * 512), np.float32)
        toks = []
        for g in range(2):
            cols = np.concatenate(
                [np.arange(128) + (16 * g + 4 * sg + j) * 128
                 for sg in range(4)])
            toks.append(cols)
            xg[g] = xT[b][:, cols].reshape(32, 128, 512).transpose(
                1, 0, 2).reshape(128, 32 * 512)
        qi = np.arange(128, dtype=np.float32)[:, None]
        qpos = np.asarray(
            [[(4 * s + j) * 128 for s in range(8)]], np.float32) + qi
        kj = np.arange(512, dtype=np.float32)[None, :]
        dmask = np.where(kj > j * 128 + qi, np.float32(-1e30),
                         np.float32(0.0)).astype(np.float32)
        m = dict(shared)
        m["xg"] = xg
        m["xkv"] = xkv[b]
        m["qpos"] = np.ascontiguousarray(qpos)
        m["dmask"] = np.ascontiguousarray(dmask)
        in_maps.append(m)
        tokmaps.append(toks)
    return in_maps, tokmaps


def kernel(**inputs):
    in_maps, tokmaps = _prep(inputs)
    nc = build_program()
    res = run_bass_kernel_spmd(nc, in_maps, core_ids=list(range(8)))
    out = np.empty((B, S, D), np.float32)
    for c in range(8):
        b = c // 4
        yt = res.results[c]["yt"]
        for g in range(2):
            blk = yt[g].reshape(128, 32, 512).transpose(1, 0, 2).reshape(
                4096, 512)
            out[b, tokmaps[c][g], :] = blk.T
    return out
